# version 14 (final)
# speedup vs baseline: 1.4831x; 1.0020x over previous
import numpy as np
from contextlib import ExitStack
import concourse.bass as bass
import concourse.mybir as mybir
from concourse.bass_utils import run_bass_kernel_spmd
import ml_dtypes

bf16 = ml_dtypes.bfloat16
F32 = mybir.dt.float32
BF16 = mybir.dt.bfloat16
AF = mybir.ActivationFunctionType
ALU = mybir.AluOpType
AXX = mybir.AxisListType.X

NCORES = 8
S, D, NH, DH = 8192, 2048, 16, 128
HFF = 8192
NKC = 16
TLOC = 1024
INVSQ = float(1.0 / np.sqrt(128.0))
EPS_LN = 1e-5
EPS_RMS = float(np.finfo(np.float32).eps)
NB = (8, 16)

SCR_H = 0
SCR_Q = 0
SCR_O = 16384
SCR_QSQ = 16384
SCR_KT = 17408
SCR_VT = 19456

AR_XB = 0
AR_XLN = 8192
AR_W = 16384
AR_XLN2 = 0
AR_QK = 8192
AR_V = 12288
AR_KB = 0
AR_VB = 16384
AR_MK = 20480
AR_WO = 0
AR_XIN = 0


class Sem:
    __slots__ = ("h", "n", "eng")

    def __init__(self, h, eng):
        self.h, self.n, self.eng = h, 0, eng


class B:
    ENGS = ("pe", "dve", "act", "gp", "sp")

    def __init__(self, nc, es):
        self.nc, self.es = nc, es
        self.sched = {k: [] for k in self.ENGS}
        self.waited = {k: {} for k in self.ENGS}
        self.psfree = {}
        self.psprod = {}
        self.alias = {}
        self.trace = {k: [] for k in self.ENGS}

    def sem(self, name, eng):
        return Sem(self.es.enter_context(self.nc.semaphore(name)), eng)

    def E(self, eng, build=None, waits=(), inc=None, k=1, tag=None):
        wl = []
        wids = []
        for w in waits:
            if w is None:
                continue
            s, v = w
            if v <= 0 or s.eng == eng:
                continue
            key = id(s)
            if self.waited[eng].get(key, 0) >= v:
                continue
            self.waited[eng][key] = v
            wl.append((s.h, v))
            wids.append((key, v))
        if inc is not None:
            inc.n += k

        def item(e, wl=wl, build=build, inc=inc, k=k):
            for (h, v) in wl:
                e.wait_ge(h, v)
            if build is not None:
                r = build(e)
                if inc is not None:
                    r.then_inc(inc.h, k)
        self.sched[eng].append(item)
        self.trace[eng].append((wids, (id(inc), k) if inc is not None else None, tag))
        return (inc, inc.n) if inc is not None else None

    def _pk(self, pst):
        return self.alias.get(id(pst), id(pst))

    def MM(self, pst, build, waits=(), start=False, final=False):
        w = list(waits)
        if start:
            w.append(self.psfree.get(self._pk(pst)))
        r = self.E("pe", build, waits=w, inc=self.sPE, tag=("w", self._pk(pst)))
        if final:
            self.psprod[self._pk(pst)] = r
        return r

    def PC(self, pst, eng, build, waits=(), inc=None):
        w = list(waits)
        w.append(self.psprod.get(self._pk(pst)))
        r = self.E(eng, build, waits=w, inc=inc if inc is not None else
                   {"dve": self.sDV, "act": self.sAC}[eng], tag=("r", self._pk(pst)))
        self.psfree[self._pk(pst)] = r
        return r


def build_nc(stage="full"):
    nc = bass.Bass("TRN2", target_bir_lowering=False, debug=False)
    dt = nc.dram_tensor

    x_loc = dt("x_loc", [TLOC, D], F32, kind="ExternalInput").ap()
    wqkv_t = dt("wqkv_t", [D, 3 * D], F32, kind="ExternalInput").ap()
    wout_t = dt("wout_t", [D, D], F32, kind="ExternalInput").ap()
    bqkv_fm = dt("bqkv_fm", [128, 48], F32, kind="ExternalInput").ap()
    bqkv_v = dt("bqkv_v", [1, D], F32, kind="ExternalInput").ap()
    bout_fm = dt("bout_fm", [128, 16], F32, kind="ExternalInput").ap()
    wgu_sl = dt("wgu_sl", [2 * HFF, 256], F32, kind="ExternalInput").ap()
    wpr_sl = dt("wpr_sl", [D, 1024], F32, kind="ExternalInput").ap()
    masks_d = dt("masks_d", [(NB[0] + NB[1]) * 128, 2048], BF16, kind="ExternalInput").ap()
    ones_c = dt("ones_c", [128, 1], F32, kind="ExternalInput").ap()
    ones_r = dt("ones_r", [1, 128], F32, kind="ExternalInput").ap()
    ident = dt("ident", [128, 128], F32, kind="ExternalInput").ap()
    cst_in = dt("cst_in", [128, 10], F32, kind="ExternalInput").ap()
    negr_in = dt("negr_in", [1, 128], F32, kind="ExternalInput").ap()
    y_out = dt("y_out", [TLOC, D], F32, kind="ExternalOutput").ap()

    ag_gu_q = [dt(f"ag_gu_q{i}", [256, 4096], BF16).ap() for i in range(4)]
    wgu_q = [dt(f"wgu_q{i}", [D, 4096], BF16, addr_space="Shared").ap() for i in range(4)]
    ag_pr_in = dt("ag_pr_in", [1024, D], BF16).ap()
    wpr_full = dt("wpr_full", [HFF, D], BF16, addr_space="Shared").ap()
    ag_k_in = dt("ag_k_in", [D, TLOC], BF16).ap()
    k_full = dt("k_full", [NCORES * D, TLOC], BF16, addr_space="Shared").ap()
    ag_v_in = dt("ag_v_in", [TLOC, D], BF16).ap()
    v_full = dt("v_full", [NCORES * TLOC, D], BF16, addr_space="Shared").ap()
    ag_n_in = dt("ag_n_in", [1, 16], F32).ap()
    kn_all = dt("kn_all", [NCORES, 16], F32, addr_space="Shared").ap()
    qn_d = dt("qn_d", [16, TLOC], F32).ap()
    wqkv_b = dt("wqkv_b", [D, 3 * D], BF16).ap()
    wout_b = dt("wout_b", [D, D], BF16).ap()

    es = ExitStack()
    with es:
        sbt = lambda name, shape, dtp: es.enter_context(nc.sbuf_tensor(name, shape, dtp))
        pst = lambda name, shape, dtp: es.enter_context(nc.psum_tensor(name, shape, dtp))

        X = sbt("X", [128, NKC * TLOC], F32)
        SCR = sbt("SCR", [128, 32768], BF16)
        AR = sbt("AR", [128, 24576], BF16)
        onc = sbt("onc", [128, 1], BF16)
        onr = sbt("onr", [1, 128], F32)
        idf = sbt("idf", [128, 128], F32)
        bqf = sbt("bqf", [128, 48], F32)
        bof = sbt("bof", [128, 16], F32)
        sgb = sbt("sgb", [128, 2 * 512], BF16)
        tmpb = sbt("tmpb", [128, 2 * 512], BF16)
        scb = sbt("scb", [128, 2 * 512], F32)
        atb = sbt("atb", [128, 4 * 512], BF16)
        bcf = sbt("bcf", [128, 2 * 512], F32)
        rowA = sbt("rowA", [1, 512], F32)
        rowB = sbt("rowB", [1, 512], F32)
        rowC = sbt("rowC", [1, 1024], F32)
        knr = sbt("knr", [1, 16], F32)
        knm = sbt("knm", [1, 16], F32)
        gsum = sbt("gsum", [128, 16], F32)
        sbq = sbt("sbq", [128, 16], BF16)
        thr = sbt("thr", [128, 16], F32)
        thrn = sbt("thrn", [128, 16], F32)
        cst = sbt("cst", [128, 10], F32)
        mh4 = sbt("mh4", [128, 4], F32)
        mhall = sbt("mhall", [128, 16], F32)
        prb = sbt("prb", [1, 64], F32)
        ngr = sbt("ngr", [1, 128], BF16)
        mr4 = sbt("mr4", [1, 4 * 512], BF16)

        ps0 = pst("ps0", [128, 512], F32)
        ps1 = pst("ps1", [128, 512], F32)
        ps2 = pst("ps2", [128, 512], F32)
        ps3 = pst("ps3", [128, 512], F32)
        psc2 = pst("psc2", [128, 1024], F32)
        ps4 = psc2[:, 0:512]
        ps5 = psc2[:, 512:1024]
        b_alias_later = (psc2, ps4, ps5)
        psS = pst("psS", [128, 512], F32)
        psT = pst("psT", [128, 128], F32)

        b = B(nc, es)
        sSP = b.sem("sSP", "dma")
        sGP = b.sem("sGP", "dma")
        sCC = b.sem("sCC", "dma")
        sPE = b.sem("sPE", "pe")
        sDV = b.sem("sDV", "dve")
        sAC = b.sem("sAC", "act")
        b.sPE, b.sDV, b.sAC = sPE, sDV, sAC
        for _t in b_alias_later:
            b.alias[id(_t)] = "PSC2"
        sCst = b.sem("sCst", "dma")
        sQW = [b.sem("sQW0", "dma"), b.sem("sQW1", "dma")]
        sQF = b.sem("sQF", "dma")
        sXin = [b.sem("sXin0", "dma"), b.sem("sXin1", "dma")]
        sGt = [b.sem("sGt0", "dma"), b.sem("sGt1", "dma")]
        sUp = [b.sem("sUp0", "dma"), b.sem("sUp1", "dma")]
        sPr = [b.sem("sPr0", "dma"), b.sem("sPr1", "dma")]
        sQk = [b.sem("sQk0", "dma"), b.sem("sQk1", "dma")]
        sVs = b.sem("sVs", "dma")
        sWo = [b.sem("sWo0", "dma"), b.sem("sWo1", "dma")]
        sKb = [b.sem("sKb0", "dma"), b.sem("sKb1", "dma")]
        sVb = [b.sem("sVb0", "dma"), b.sem("sVb1", "dma")]
        sMk = [b.sem("sMk0", "dma"), b.sem("sMk1", "dma")]
        sStK = b.sem("sStK", "dma")
        sStV = b.sem("sStV", "dma")
        sStY = b.sem("sStY", "dma")
        sWC = b.sem("sWC", "dma")
        sKt = [b.sem(f"sKt{i}", "dma") for i in range(4)]
        sVt = [b.sem(f"sVt{i}", "dma") for i in range(4)]

        RU = {}
        dbgi = []

        def r_wait(key):
            return RU.get(key)

        def r_set(key, evt):
            RU[key] = evt

        SCRf = SCR.ap().bitcast(F32)
        ARf = AR.ap().bitcast(F32)

        def gdma(out, in_, waits=(), sem=None):
            return b.E("gp", lambda e, o=out, i=in_: e.dma_start(out=o, in_=i), waits=waits,
                       inc=sem if sem is not None else sGP, k=16)

        def sdma(out, in_, waits=(), sem=None):
            return b.E("sp", lambda e, o=out, i=in_: e.dma_start(out=o, in_=i), waits=waits,
                       inc=sem if sem is not None else sSP, k=16)

        def cur(sem):
            return (sem, sem.n)

        gdma(onc[:, :], ones_c[:, :], sem=sCst)
        sdma(cst[:, :], cst_in[:, :], sem=sCst)
        gdma(ngr[:, :], negr_in[:, :], sem=sCst)
        sdma(onr[:, :], ones_r[:, :], sem=sCst)
        sdma(idf[:, :], ident[:, :], sem=sCst)
        sdma(bqf[:, :], bqkv_fm[:, :], sem=sCst)
        sdma(bof[:, :], bout_fm[:, :], sem=sCst)
        consts_sp = (sCst, sCst.n)
        consts_gp = (sCst, sCst.n)

        def emit_quant(src_dram, n_tiles, cols, ag_dst, tile_order=None, marks=None, blk=1):
            ngroups = blk * cols // 64
            nch = cols // 128
            wcols = blk * cols
            qtile_use = [None, None]
            whf_use = [None, None]
            tr_use = {}
            order = tile_order if tile_order is not None else list(range(n_tiles))
            assert n_tiles % blk == 0
            for bi in range(n_tiles // blk):
                ts_ = order[bi * blk:(bi + 1) * blk]
                assert ts_ == list(range(ts_[0], ts_[0] + blk))
                par = bi % 2
                wt = SCR[:, par * 1024: par * 1024 + wcols]
                whf = SCRf[:, 1024 + par * 1024: 1024 + par * 1024 + wcols]
                mneg = SCRf[:, 3072 + par * 1024: 3072 + par * 1024 + wcols]
                lw = gdma(wt.rearrange("p (a c) -> p a c", c=cols) if blk > 1 else wt,
                          src_dram[ts_[0] * 128: ts_[0] * 128 + blk * 128, :]
                          .rearrange("(a p) c -> p a c", p=128) if blk > 1 else
                          src_dram[ts_[0] * 128:(ts_[0] + 1) * 128, :],
                          waits=(qtile_use[par], consts_gp), sem=sQW[par])
                wt3 = wt.rearrange("p (g u) -> p g u", u=64)
                whf3 = whf.rearrange("p (g u) -> p g u", u=64)
                mneg3 = mneg.rearrange("p (g u) -> p g u", u=64)

                def bc(tt, ng=ngroups):
                    return tt[:, 0:ng].unsqueeze(2).broadcast_to([128, ng, 64])
                b.E("dve", lambda e, wt3=wt3, ng=ngroups: e.tensor_reduce(
                    gsum[:, 0:ng], wt3, axis=AXX, op=ALU.add, apply_absolute_value=True),
                    waits=(lw, consts_sp), inc=sDV)
                b.E("dve", lambda e, ng=ngroups: e.tensor_scalar(
                    out=sbq[:, 0:ng], in0=gsum[:, 0:ng], scalar1=cst[:, 0:1], scalar2=cst[:, 1:2],
                    op0=ALU.mult, op1=ALU.max), inc=sDV)
                b.E("dve", lambda e, ng=ngroups: e.tensor_scalar(
                    out=thr[:, 0:ng], in0=sbq[:, 0:ng], scalar1=cst[:, 2:3], scalar2=None,
                    op0=ALU.mult), inc=sDV)
                b.E("dve", lambda e, ng=ngroups: e.tensor_scalar(
                    out=thrn[:, 0:ng], in0=sbq[:, 0:ng], scalar1=cst[:, 3:4], scalar2=None,
                    op0=ALU.mult), inc=sDV)
                b.E("dve", lambda e, whf3=whf3, wt3=wt3, bc=bc: e.tensor_tensor(
                    whf3, wt3, bc(thr), op=ALU.is_gt), waits=(whf_use[par],), inc=sDV)
                b.E("dve", lambda e, mneg3=mneg3, wt3=wt3, bc=bc: e.tensor_tensor(
                    mneg3, wt3, bc(thrn), op=ALU.is_lt), inc=sDV)
                b.E("dve", lambda e, whf3=whf3, mneg3=mneg3: e.tensor_tensor(
                    whf3, whf3, mneg3, op=ALU.subtract), inc=sDV)
                wq = b.E("dve", lambda e, whf3=whf3, bc=bc: e.tensor_tensor(
                    whf3, whf3, bc(sbq), op=ALU.mult), inc=sDV)
                qtile_use[par] = wq
                tw = None
                for a in range(blk):
                    t = ts_[a]
                    ti = bi * blk + a
                    g, fl = divmod(ti, 8)
                    gp_ = g % 2
                    for cc in range(nch):
                        troff = 10240 + (gp_ * nch + cc) * 1024 + fl * 128
                        tw = b.MM(psT, lambda e, whf=whf, cc=cc, a=a: e.transpose(
                            psT[:, :], whf[:, a * cols + cc * 128: a * cols + (cc + 1) * 128],
                            idf[:, :]),
                            waits=(wq, consts_sp, cur(sQF) if fl == 0 and cc == 0 else None),
                            start=True, final=True)
                        b.PC(psT, "dve", lambda e, troff=troff: e.tensor_copy(
                            SCR[:, troff: troff + 128], psT[:, :]))
                    if fl == 7:
                        fin = (sDV, sDV.n)
                        dst_t, colbase = ag_dst(g)
                        for cc in range(nch):
                            troff = 10240 + (gp_ * nch + cc) * 1024
                            fw = gdma(dst_t[cc * 128:(cc + 1) * 128, colbase: colbase + 1024],
                                      SCR[:, troff: troff + 1024], waits=(fin,), sem=sQF)
                            tr_use[(gp_, cc)] = fw
                        if marks is not None:
                            marks[g] = (sQF, sQF.n)
                whf_use[par] = tw

        full = stage in ("quant", "ffn_dbg", "ffn1", "attn", "attn_dbg", "full")
        ag1_gate = {}
        ag1done = (sCC, 0)
        if full:
            order = list(range(0, 32)) + list(range(64, 96)) + list(range(32, 64)) + list(range(96, 128))
            gmarks = {}

            def gu_dst(g):
                t0 = order[8 * g]
                quarter = t0 // 32
                colbase = (t0 % 32) * 128 // 1024 * 1024
                return ag_gu_q[quarter], ((t0 * 128) % 4096)

            emit_quant(wgu_sl, 128, 256, gu_dst, tile_order=order, marks=gmarks, blk=4)
            pmarks = {}
            emit_quant(wpr_sl, 16, 1024, lambda g: (ag_pr_in, g * 1024), marks=pmarks)
            gate_gp = (sGP, sGP.n)
            for qi_, lastg in ((0, 3), (2, 7), (1, 11), (3, 15)):
                b.E("gp", lambda e, qi_=qi_: e.collective_compute(
                    "AllGather", ALU.bypass, replica_groups=[list(range(NCORES))],
                    ins=[ag_gu_q[qi_]], outs=[wgu_q[qi_]]),
                    waits=(gmarks[lastg], gate_gp), inc=sCC)
                ag1_gate[qi_] = (sCC, sCC.n)
            b.E("gp", lambda e: e.collective_compute(
                "AllGather", ALU.bypass, replica_groups=[list(range(NCORES))],
                ins=[ag_pr_in], outs=[wpr_full]), waits=(pmarks[max(pmarks)],), inc=sCC)
            ag1done = (sCC, sCC.n)

        if stage == "quant":
            dbg_gu = dt("dbg_gu", [D, 2 * HFF], BF16, kind="ExternalOutput").ap()
            dbg_pr = dt("dbg_pr", [HFF, D], BF16, kind="ExternalOutput").ap()
            d1 = gdma(dbg_gu[:, :], wgu_full[:, :], waits=(ag1done,))
            d2 = gdma(dbg_pr[:, :], wpr_full[:, :], waits=(ag1done,))

        if full:
            for i in range(4):
                gdma(wqkv_b[i * 512:(i + 1) * 512, :], wqkv_t[i * 512:(i + 1) * 512, :],
                     waits=(ag1done if i == 0 else None,), sem=sWC)
            for i in range(2):
                gdma(wout_b[i * 1024:(i + 1) * 1024, :], wout_t[i * 1024:(i + 1) * 1024, :], sem=sWC)
        wcast_done = (sWC, sWC.n)

        xin_use = [None, None]
        PS4X = [ps0, ps1, ps2, ps3]
        for tt in range(8):
            par = tt % 2
            xin = ARf[:, par * 2048:(par + 1) * 2048]
            lx = sdma(xin, x_loc[tt * 128:(tt + 1) * 128, :], waits=(xin_use[par], consts_sp),
                      sem=sXin[par])
            last = None
            for g4 in range(4):
                pg4 = PS4X[g4]
                for i in range(4):
                    kc = g4 * 4 + i
                    b.MM(pg4, lambda e, xin=xin, kc=kc, pg4=pg4, i=i: e.transpose(
                        pg4[:, i * 128:(i + 1) * 128], xin[:, kc * 128:(kc + 1) * 128], idf[:, :]),
                        waits=(lx,), start=(i == 0), final=(i == 3))
                last = b.PC(pg4, "dve", lambda e, g4=g4, tt=tt, pg4=pg4: e.tensor_copy(
                    X[:, g4 * 4 * TLOC: (g4 * 4 + 4) * TLOC]
                    .rearrange("p (c t) -> p c t", c=4)[:, :, tt * 128: tt * 128 + 128],
                    pg4.ap().rearrange("p (c u) -> p c u", c=4) if hasattr(pg4, "ap") else
                    pg4.rearrange("p (c u) -> p c u", c=4)))
            xin_use[par] = last
        x_loaded = (sDV, sDV.n)

        def emit_ln(tg, xb_off, xln_off, prev):
            XBv = AR[:, xb_off: xb_off + 8192]
            XLNv = AR[:, xln_off: xln_off + 8192]
            xsq_use = [None, None]
            for kc in range(NKC):
                par = kc % 2
                xbc = XBv[:, kc * 512:(kc + 1) * 512]
                xc = X[:, kc * TLOC + tg * 512: kc * TLOC + tg * 512 + 512]
                c1 = b.E("dve", lambda e, xbc=xbc, xc=xc: e.tensor_copy(xbc, xc),
                         waits=(prev, consts_sp, consts_gp), inc=sDV)
                b.MM(ps4, lambda e, xbc=xbc, kc=kc: e.matmul(
                    ps4[0:1, :], onc[:, :], xbc, start=(kc == 0), stop=(kc == NKC - 1)),
                    waits=(c1, consts_gp), start=(kc == 0), final=(kc == NKC - 1))
                sq = b.E("dve", lambda e, xbc=xbc, par=par: e.tensor_tensor(
                    tmpb[:, par * 512:(par + 1) * 512], xbc, xbc, op=ALU.mult),
                    waits=(xsq_use[par],), inc=sDV)
                m2 = b.MM(ps5, lambda e, par=par, kc=kc: e.matmul(
                    ps5[0:1, :], onc[:, :], tmpb[:, par * 512:(par + 1) * 512],
                    start=(kc == 0), stop=(kc == NKC - 1)),
                    waits=(sq,), start=(kc == 0), final=(kc == NKC - 1))
                xsq_use[par] = m2
            b.PC(ps4, "dve", lambda e: e.tensor_scalar(
                out=rowA[:, :], in0=ps4[0:1, :], scalar1=cst[0:1, 4:5], scalar2=None, op0=ALU.mult),
                waits=(r_wait("rowA"),))
            b.PC(ps5, "dve", lambda e: e.tensor_scalar(
                out=rowB[:, :], in0=ps5[0:1, :], scalar1=cst[0:1, 4:5], scalar2=None, op0=ALU.mult),
                waits=(r_wait("rowB"),))
            b.E("dve", lambda e: e.tensor_tensor(rowC[:, 0:512], rowA[:, :], rowA[:, :], op=ALU.mult),
                waits=(r_wait("rowC"),), inc=sDV)
            b.E("dve", lambda e: e.tensor_tensor(rowB[:, :], rowB[:, :], rowC[:, 0:512], op=ALU.subtract),
                inc=sDV)
            b.E("dve", lambda e: e.tensor_scalar(out=rowB[:, :], in0=rowB[:, :], scalar1=cst[0:1, 6:7],
                                                 scalar2=None, op0=ALU.add), inc=sDV)
            rv = b.E("dve", lambda e: e.reciprocal(rowB[:, :], rowB[:, :]), inc=sDV)
            sv = b.E("act", lambda e: e.activation(out=rowB[:, :], in_=rowB[:, :], func=AF.Sqrt),
                     waits=(rv,), inc=sAC)
            nm = b.E("dve", lambda e: e.tensor_scalar(out=rowC[:, 0:512], in0=rowA[:, :],
                                                      scalar1=cst[0:1, 5:6],
                                                      scalar2=None, op0=ALU.mult), inc=sDV)
            nm2 = b.E("dve", lambda e: e.tensor_tensor(rowC[:, 0:512], rowC[:, 0:512], rowB[:, :],
                                                       op=ALU.mult), waits=(sv,), inc=sDV)
            bm1 = b.MM(ps2, lambda e: e.matmul(ps2[:, :], onr[:, :], rowB[:, :], start=True, stop=True),
                       waits=(sv, consts_sp), start=True, final=True)
            bm2 = b.MM(ps3, lambda e: e.matmul(ps3[:, :], onr[:, :], rowC[0:1, 0:512], start=True, stop=True),
                       waits=(nm2,), start=True, final=True)
            r_set("rowB", bm1)
            r_set("rowC", bm2)
            bi = b.PC(ps2, "dve", lambda e: e.tensor_copy(bcf[:, 0:512], ps2[:, :]))
            bn = b.PC(ps3, "dve", lambda e: e.tensor_copy(bcf[:, 512:1024], ps3[:, :]))
            last = None
            for kc in range(NKC):
                xc = X[:, kc * TLOC + tg * 512: kc * TLOC + tg * 512 + 512]
                xo = XLNv[:, kc * 512:(kc + 1) * 512]
                b.E("dve", lambda e, xc=xc: e.tensor_tensor(scb[:, 0:512], xc, bcf[:, 0:512],
                                                            op=ALU.mult), waits=(bi, bn), inc=sDV)
                last = b.E("dve", lambda e, xo=xo: e.tensor_tensor(xo, scb[:, 0:512], bcf[:, 512:1024],
                                                                   op=ALU.add), inc=sDV)
            return last

        def emit_ffn(tg, xln_done, arena_free=None):
            XLNv = AR[:, AR_XLN: AR_XLN + 8192]
            pg = [ps0, ps2]
            pu = [ps1, ps3]
            pe_snap = (sPE, sPE.n)
            wg_use = [pe_snap, pe_snap, pe_snap, pe_snap]
            sgb_use = [None, None]
            hsq_use = [None, None]
            h_written = None
            for mb in range(64):
                par = mb % 2
                gsl = AR[:, AR_W + par * 2048: AR_W + par * 2048 + 2048]
                usl = AR[:, AR_W + 4096 + par * 2048: AR_W + 4096 + par * 2048 + 2048]
                qtr = mb // 32
                lg = sdma(gsl.rearrange("p (kc h) -> p kc h", h=128),
                          wgu_q[qtr][:, (mb % 32) * 128:(mb % 32 + 1) * 128]
                          .rearrange("(kc p) h -> p kc h", p=128),
                          waits=(wg_use[par], ag1_gate.get(qtr)), sem=sGt[par])
                uqtr = 2 + mb // 32
                lu = sdma(usl.rearrange("p (kc h) -> p kc h", h=128),
                          wgu_q[uqtr][:, (mb % 32) * 128:(mb % 32 + 1) * 128]
                          .rearrange("(kc p) h -> p kc h", p=128),
                          waits=(wg_use[2 + par], ag1_gate.get(uqtr)), sem=sUp[par])
                mmg = mmu = None
                for kc in range(NKC):
                    mmg = b.MM(pg[par], lambda e, gsl=gsl, kc=kc, par=par: e.matmul(
                        pg[par][:, :], gsl[:, kc * 128:(kc + 1) * 128],
                        XLNv[:, kc * 512:(kc + 1) * 512],
                        start=(kc == 0), stop=(kc == NKC - 1)),
                        waits=(lg, xln_done), start=(kc == 0), final=(kc == NKC - 1))
                for kc in range(NKC):
                    mmu = b.MM(pu[par], lambda e, usl=usl, kc=kc, par=par: e.matmul(
                        pu[par][:, :], usl[:, kc * 128:(kc + 1) * 128],
                        XLNv[:, kc * 512:(kc + 1) * 512],
                        start=(kc == 0), stop=(kc == NKC - 1)),
                        waits=(lu,), start=(kc == 0), final=(kc == NKC - 1))
                wg_use[par] = mmg
                wg_use[2 + par] = mmu
                sg = b.PC(pg[par], "act", lambda e, par=par: e.activation(
                    out=sgb[:, par * 512:(par + 1) * 512], in_=pg[par][:, :], func=AF.Silu),
                    waits=(sgb_use[par],))
                hc = b.PC(pu[par], "dve", lambda e, par=par, mb=mb: e.tensor_tensor(
                    SCR[:, SCR_H + mb * 512: SCR_H + (mb + 1) * 512],
                    sgb[:, par * 512:(par + 1) * 512], pu[par][:, :], op=ALU.mult),
                    waits=(sg,))
                sgb_use[par] = hc
                h_written = hc
                hq = b.E("dve", lambda e, par=par, mb=mb: e.tensor_tensor(
                    tmpb[:, par * 512:(par + 1) * 512],
                    SCR[:, SCR_H + mb * 512: SCR_H + (mb + 1) * 512],
                    SCR[:, SCR_H + mb * 512: SCR_H + (mb + 1) * 512], op=ALU.mult),
                    waits=(hsq_use[par],), inc=sDV)
                m3 = b.MM(psS, lambda e, par=par, mb=mb: e.matmul(
                    psS[0:1, :], onc[:, :], tmpb[:, par * 512:(par + 1) * 512],
                    start=(mb == 0), stop=(mb == 63)),
                    waits=(hq,), start=(mb == 0), final=(mb == 63))
                hsq_use[par] = m3
            b.PC(psS, "dve", lambda e: e.tensor_scalar(
                out=rowA[:, :], in0=psS[0:1, :], scalar1=cst[0:1, 7:8], scalar2=cst[0:1, 8:9],
                op0=ALU.mult, op1=ALU.add), waits=(r_wait("rowA"),))
            rv = b.E("dve", lambda e: e.reciprocal(rowA[:, :], rowA[:, :]), inc=sDV)
            sv = b.E("act", lambda e: e.activation(out=rowA[:, :], in_=rowA[:, :], func=AF.Sqrt,
                                                   scale=0.25), waits=(rv,), inc=sAC)
            bm = b.MM(ps2, lambda e: e.matmul(ps2[:, :], onr[:, :], rowA[:, :], start=True, stop=True),
                      waits=(sv,), start=True, final=True)
            r_set("rowA", bm)
            sbs = b.PC(ps2, "dve", lambda e: e.tensor_copy(bcf[:, 0:512], ps2[:, :]))
            py = [ps0, ps1]
            wp_use = {0: wg_use[1], 1: wg_use[3]}
            wp_use0 = {0: wg_use[0], 1: wg_use[2]}
            last = None
            for ob in range(16):
                par = ob % 2
                mmv = None
                for half in range(2):
                    wsl = AR[:, AR_W + half * 4096: AR_W + half * 4096 + 4096]
                    lw = sdma(wsl.rearrange("p (hc d) -> p hc d", d=128),
                              wpr_full[half * 4096:(half + 1) * 4096, ob * 128:(ob + 1) * 128]
                              .rearrange("(hc p) d -> p hc d", p=128),
                              waits=(wp_use.get(half), wp_use0.pop(half, None), ag1done),
                              sem=sPr[half])
                    for hc in range(32):
                        hch = half * 32 + hc
                        mmv = b.MM(py[par], lambda e, wsl=wsl, hc=hc, hch=hch, par=par: e.matmul(
                            py[par][:, :], wsl[:, hc * 128:(hc + 1) * 128],
                            SCR[:, SCR_H + hch * 512: SCR_H + (hch + 1) * 512],
                            start=(hch == 0), stop=(hch == 63)),
                            waits=(lw, h_written), start=(hch == 0), final=(hch == 63))
                    wp_use[half] = mmv
                t1 = b.PC(py[par], "dve", lambda e, par=par: e.tensor_tensor(
                    scb[:, 0:512], py[par][:, :], bcf[:, 0:512], op=ALU.mult),
                    waits=(sbs,))
                last = b.E("dve", lambda e, ob=ob, tg=tg: e.tensor_tensor(
                    X[:, ob * TLOC + tg * 512: ob * TLOC + tg * 512 + 512],
                    X[:, ob * TLOC + tg * 512: ob * TLOC + tg * 512 + 512],
                    scb[:, 0:512], op=ALU.add), inc=sDV)
            return last

        if stage == "io":
            final_dep = x_loaded
        elif stage == "ffn_dbg":
            dbg_xln = dt("dbg_xln", [128, 8192], BF16, kind="ExternalOutput").ap()
            dbg_h = dt("dbg_h", [128, 32768], BF16, kind="ExternalOutput").ap()
            dbg_s = dt("dbg_s", [128, 512], F32, kind="ExternalOutput").ap()
            ln = emit_ln(0, AR_XB, AR_XLN, x_loaded)
            gdma(dbg_xln[:, :], AR[:, AR_XLN: AR_XLN + 8192], waits=(ln,))
            f = emit_ffn(0, ln)
            gdma(dbg_h[:, :], SCR[:, 0:32768], waits=(f,))
            gdma(dbg_s[:, :], bcf[:, 0:512], waits=(f,))
            final_dep = f
        else:
            dep = x_loaded
            for tg in range(2):
                ln = emit_ln(tg, AR_XB, AR_XLN, dep)
                dep = emit_ffn(tg, ln)
            final_dep = dep

            if stage in ("attn", "attn_dbg", "full"):
                XLN2 = AR[:, AR_XLN2: AR_XLN2 + 8192]
                pqk = [ps0, ps1]
                pv = [ps2, ps3]
                qk_use = [None, None]
                v_use = [None]
                kt_norm_use = [None, None]
                ktmp_use = [None, None, None, None]
                vtmp_use = [None, None, None, None]
                qn_mm_last = None
                kn_ready = []
                def qk_block(fb, tg, ln2, qk_use):
                    par = fb % 2
                    qsl = AR[:, AR_QK + par * 2048: AR_QK + par * 2048 + 2048]
                    lq = sdma(qsl.rearrange("p (kc f) -> p kc f", f=128),
                              wqkv_b[:, fb * 128:(fb + 1) * 128].rearrange("(kc p) f -> p kc f", p=128),
                              waits=(qk_use[par], ln2, wcast_done), sem=sQk[par])
                    mmq = None
                    for kc in range(NKC):
                        mmq = b.MM(pqk[par], lambda e, qsl=qsl, kc=kc, par=par: e.matmul(
                            pqk[par][:, :], qsl[:, kc * 128:(kc + 1) * 128],
                            XLN2[:, kc * 512:(kc + 1) * 512],
                            start=(kc == 0), stop=(kc == NKC - 1)),
                            waits=(lq, ln2), start=(kc == 0), final=(kc == NKC - 1))
                    qk_use[par] = mmq
                    if fb < 16:
                        h = fb
                        b.PC(pqk[par], "dve", lambda e, par=par, h=h, tg=tg: e.tensor_scalar(
                            out=SCR[:, SCR_Q + h * 1024 + tg * 512: SCR_Q + h * 1024 + tg * 512 + 512],
                            in0=pqk[par][:, :], scalar1=bqf[:, h:h + 1], scalar2=INVSQ,
                            op0=ALU.add, op1=ALU.mult), waits=(consts_sp,))
                    else:
                        h = fb - 16
                        par2 = h % 4
                        kt = SCR[:, SCR_KT + par2 * 512: SCR_KT + par2 * 512 + 512]
                        ke = b.PC(pqk[par], "dve", lambda e, par=par, kt=kt, h=h: e.tensor_scalar(
                            out=kt, in0=pqk[par][:, :], scalar1=bqf[:, 16 + h:16 + h + 1],
                            scalar2=None, op0=ALU.add), waits=(cur(sKt[par2]), consts_sp))
                        gdma(ag_k_in[h * 128:(h + 1) * 128, tg * 512:(tg + 1) * 512], kt,
                             waits=(ke,), sem=sKt[par2])
                for tg in range(2):
                    ln2 = emit_ln(tg, AR_QK, AR_XLN2, dep)
                    for i, fb in enumerate(range(0, 32)):
                        qk_block(fb, tg, ln2 if i < 2 else None, qk_use)
                    for vb in range(4):
                        vsl = AR[:, AR_V: AR_V + 8192]
                        lv = sdma(vsl.rearrange("p (kc v) -> p kc v", v=512),
                                  wqkv_b[:, 2 * D + vb * 512: 2 * D + (vb + 1) * 512]
                                  .rearrange("(kc p) v -> p kc v", p=128),
                                  waits=(v_use[0], ln2 if vb == 0 else None, wcast_done), sem=sVs)
                        lb = gdma(rowA[:, :], bqkv_v[0:1, vb * 512:(vb + 1) * 512],
                                  waits=(r_wait("rowA"),))
                        bmv = b.MM(ps5, lambda e: e.matmul(ps5[:, :], onr[:, :], rowA[:, :],
                                                           start=True, stop=True),
                                   waits=(lb,), start=True, final=True)
                        r_set("rowA", bmv)
                        vbb = b.PC(ps5, "dve", lambda e: e.tensor_copy(bcf[:, 512:1024], ps5[:, :]))
                        mmv = None
                        for ttl in range(4):
                            parv = ttl % 2
                            for kc in range(NKC):
                                mmv = b.MM(pv[parv], lambda e, kc=kc, ttl=ttl, parv=parv, vsl=vsl, tg=tg:
                                           e.matmul(pv[parv][:, :],
                                                    XLN2[:, kc * 512 + ttl * 128: kc * 512 + ttl * 128 + 128],
                                                    vsl[:, kc * 512:(kc + 1) * 512],
                                                    start=(kc == 0), stop=(kc == NKC - 1)),
                                           waits=(lv,), start=(kc == 0), final=(kc == NKC - 1))
                            parw = ttl % 4
                            vt = SCR[:, SCR_VT + parw * 512: SCR_VT + parw * 512 + 512]
                            ve = b.PC(pv[parv], "dve", lambda e, vt=vt, parv=parv: e.tensor_tensor(
                                vt, pv[parv][:, :], bcf[:, 512:1024], op=ALU.add),
                                waits=(vbb, cur(sVt[parw])))
                            vw = gdma(ag_v_in[tg * 512 + ttl * 128: tg * 512 + ttl * 128 + 128,
                                              vb * 512:(vb + 1) * 512], vt, waits=(ve,), sem=sVt[parw])
                            vtmp_use[parw] = vw
                        v_use[0] = mmv
                    dep = (sDV, sDV.n)

                pn2 = [ps2, ps3]
                nq_done = dep
                qsq_use = [None, None]
                for idx in range(64):
                    kind = idx % 2
                    hh = (idx // 2) % 16
                    tg_ = (idx // 2) // 16
                    parn = idx % 2
                    pnb = pn2[parn]
                    if kind == 0:
                        qs = b.E("dve", lambda e, hh=hh, tg_=tg_, parn=parn: e.tensor_tensor(
                            SCR[:, SCR_QSQ + parn * 512: SCR_QSQ + parn * 512 + 512],
                            SCR[:, SCR_Q + hh * 1024 + tg_ * 512: SCR_Q + hh * 1024 + tg_ * 512 + 512],
                            SCR[:, SCR_Q + hh * 1024 + tg_ * 512: SCR_Q + hh * 1024 + tg_ * 512 + 512],
                            op=ALU.mult), waits=(nq_done, qsq_use[parn]), inc=sDV)
                    else:
                        klr = gdma(SCR[:, SCR_KT + parn * 512: SCR_KT + parn * 512 + 512],
                                   ag_k_in[hh * 128:(hh + 1) * 128, tg_ * 512:(tg_ + 1) * 512],
                                   waits=(cur(sKt[0]), cur(sKt[1]), cur(sKt[2]), cur(sKt[3]),
                                          kt_norm_use[parn]))
                        qs = b.E("dve", lambda e, parn=parn: e.tensor_tensor(
                            SCR[:, SCR_QSQ + parn * 512: SCR_QSQ + parn * 512 + 512],
                            SCR[:, SCR_KT + parn * 512: SCR_KT + parn * 512 + 512],
                            SCR[:, SCR_KT + parn * 512: SCR_KT + parn * 512 + 512],
                            op=ALU.mult), waits=(klr, qsq_use[parn]), inc=sDV)
                    mn = b.MM(pnb, lambda e, pnb=pnb, parn=parn: e.matmul(
                        pnb[0:1, :], onc[:, :], SCR[:, SCR_QSQ + parn * 512: SCR_QSQ + parn * 512 + 512],
                        start=True, stop=True), waits=(qs,), start=True, final=True)
                    qsq_use[parn] = mn
                    if kind == 1:
                        kt_norm_use[parn] = mn
                    if kind == 0:
                        cp = b.PC(pnb, "dve", lambda e, pnb=pnb: e.tensor_copy(rowA[:, :], pnb[0:1, :]),
                                  waits=(r_wait("rowA"),))
                        r_set("rowA", gdma(qn_d[hh:hh + 1, tg_ * 512:(tg_ + 1) * 512], rowA[:, :],
                                           waits=(cp,)))
                    else:
                        cp = b.PC(pnb, "dve", lambda e, pnb=pnb: e.tensor_copy(rowB[:, :], pnb[0:1, :]),
                                  waits=(r_wait("rowB"),))
                        if tg_ == 0:
                            kn_ready.append(b.E("dve", lambda e, hh=hh: e.reduce_max(
                                knr[0:1, hh:hh + 1], rowB[:, :], axis=AXX), inc=sDV))
                        else:
                            b.E("dve", lambda e: e.reduce_max(rowC[0:1, 0:1], rowB[:, :], axis=AXX),
                                waits=(r_wait("rowC"),), inc=sDV)
                            kn_ready.append(b.E("dve", lambda e, hh=hh: e.tensor_tensor(
                                knr[0:1, hh:hh + 1], knr[0:1, hh:hh + 1], rowC[0:1, 0:1], op=ALU.max),
                                inc=sDV))
                dep = (sDV, sDV.n)

                knw = gdma(ag_n_in[:, :], knr[:, :], waits=(kn_ready[-1],))
                ag2c = (sGP, sGP.n)
                b.E("gp", lambda e: e.collective_compute(
                    "AllGather", ALU.bypass, replica_groups=[list(range(NCORES))],
                    ins=[ag_k_in], outs=[k_full]),
                    waits=(ag2c, cur(sKt[0]), cur(sKt[1]), cur(sKt[2]), cur(sKt[3]),
                           cur(sVt[0]), cur(sVt[1]), cur(sVt[2]), cur(sVt[3])), inc=sCC)
                b.E("gp", lambda e: e.collective_compute(
                    "AllGather", ALU.bypass, replica_groups=[list(range(NCORES))],
                    ins=[ag_v_in], outs=[v_full]), inc=sCC)
                b.E("gp", lambda e: e.collective_compute(
                    "AllGather", ALU.bypass, replica_groups=[list(range(NCORES))],
                    ins=[ag_n_in], outs=[kn_all]), inc=sCC)
                ag2done = (sCC, sCC.n)
                qkv_done = dep
                qn_written = (sGP, sGP.n)
                lk = b.E("sp", lambda e: e.dma_start(
                    out=rowC[:, 0:128].rearrange("o (h r) -> o h r", r=8),
                    in_=kn_all.rearrange("r h -> h r").unsqueeze(0),
                    allow_slow_non_contiguous=True),
                    waits=(ag2done, r_wait("rowC")), inc=sSP, k=16)
                rm = b.E("dve", lambda e: e.tensor_reduce(
                    knm[:, :], rowC[:, 0:128].rearrange("o (h r) -> o h r", r=8),
                    axis=AXX, op=ALU.max), waits=(lk,), inc=sDV)
                knm_done = b.E("dve", lambda e: e.tensor_scalar(
                    out=knm[:, :], in0=knm[:, :], scalar1=cst[0:1, 9:10], scalar2=None,
                    op0=ALU.mult), inc=sDV)

                if stage == "attn_dbg":
                    dbg_knm = dt("dbg_knm", [1, 16], F32, kind="ExternalOutput").ap()
                    dbg_qn = dt("dbg_qn", [16, TLOC], F32, kind="ExternalOutput").ap()
                    b2 = b.E("gp", lambda e: e.dma_start(out=dbg_knm[:, :], in_=knm[:, :]),
                             waits=(knm_done,), inc=sGP, k=16)
                    b3 = gdma(dbg_qn[:, :], qn_d[:, :], waits=(qn_written,))
                PO = [ps0, ps1, ps2, ps3]
                PSC = [ps4, ps5]
                kb_use = [None, None]
                vb_use = [None, None]
                mk_use = [None, None]
                atb_use = [None, None]
                rot = [0]
                o_written = None
                pend = []

                def emit_back(entry):
                    slot, hp, st_flag, sp_flag, mk, lvb, VB, parj, par = entry
                    lastav = None
                    for half in range(2):
                        hi = hp * 2 + half
                        asl = atb[:, slot * 1024 + half * 512: slot * 1024 + half * 512 + 512]
                        b.MM(psS, lambda e, hi=hi, asl=asl, st=st_flag, sp=sp_flag:
                             e.matmul(psS[32 * hi:32 * hi + 1, :], onc[:, :], asl,
                                      start=st, stop=sp, tile_position=(0, 32 * hi)),
                             waits=(mk,), start=(st_flag and hi == 0),
                             final=(sp_flag and hi == 3))
                    for half in range(2):
                        hi = hp * 2 + half
                        h = cur_hg[0] * 4 + hi
                        asl = atb[:, slot * 1024 + half * 512: slot * 1024 + half * 512 + 512]
                        lastav = b.MM(PO[hi], lambda e, hi=hi, asl=asl, h=h, VB=VB,
                                      st=st_flag, sp=sp_flag:
                                      e.matmul(PO[hi][:, :], VB[:, h * 128:(h + 1) * 128], asl,
                                               start=st, stop=sp),
                                      waits=(lvb, mk), start=st_flag, final=sp_flag)
                    atb_use[slot] = lastav
                    back_last[0] = lastav
                    vb_use[parj] = lastav
                    kb_use[par] = lastav
                    mk_use[par] = lastav
                    return lastav

                cur_hg = [0]
                back_last = [None]
                mball_done = None
                for qi in range(2):
                    for hg in range(4):
                        for hp in range(2):
                            for half in range(2):
                                h = hg * 4 + hp * 2 + half
                                lq = gdma(rowA[:, :], qn_d[h:h + 1, qi * 512:(qi + 1) * 512],
                                          waits=(qn_written, ag2done, r_wait("rowA")))
                                mxq = b.E("dve", lambda e, half=half: e.reduce_max(
                                    rowC[0:1, half:half + 1], rowA[:, :], axis=AXX),
                                    waits=(lq, knm_done, r_wait("rowC")), inc=sDV)
                                r_set("rowA", mxq)
                                b.E("dve", lambda e, half=half, h=h: e.tensor_scalar(
                                    out=rowC[0:1, half:half + 1], in0=rowC[0:1, half:half + 1],
                                    scalar1=knm[0:1, h:h + 1], scalar2=None, op0=ALU.mult),
                                    inc=sDV)
                            pmx = b.E("dve", lambda e: e.tensor_tensor(
                                rowC[0:1, 0:1], rowC[0:1, 0:1], rowC[0:1, 1:2], op=ALU.max),
                                inc=sDV)
                            b.E("dve", lambda e: e.tensor_copy(prb[0:1, 32:33],
                                                               rowC[0:1, 0:1]), inc=sDV)
                            tch = b.E("dve", lambda e: e.tensor_copy(prb[0:1, 33:34],
                                                                     rowC[0:1, 0:1]), inc=sDV)
                            sq = b.E("act", lambda e: e.activation(
                                out=rowC[0:1, 2:3], in_=rowC[0:1, 0:1], func=AF.Sqrt),
                                waits=(tch,), inc=sAC)
                            ng2 = b.E("dve", lambda e: e.tensor_scalar(
                                out=rowC[0:1, 0:1], in0=rowC[0:1, 2:3], scalar1=cst[0:1, 5:6],
                                scalar2=None, op0=ALU.mult), waits=(sq,), inc=sDV)
                            bmh = b.MM(ps4, lambda e: e.matmul(
                                ps4[:, :], onr[:, :], rowC[0:1, 0:512], start=True, stop=True),
                                waits=(ng2,), start=True, final=True)
                            r_set("rowC", bmh)
                            gidx = (qi * 4 + hg) * 2 + hp
                            mball_done = b.PC(ps4, "dve", lambda e, gidx=gidx: e.tensor_copy(
                                mhall[:, gidx:gidx + 1], ps4[:, 0:1]))
                for qi in range(2):
                    nb = NB[qi]
                    for hg in range(4):
                        cur_hg[0] = hg
                        gcol = (qi * 4 + hg) * 2
                        for kb in range(nb):
                            r, off = (kb, 0) if kb < 8 else (15 - kb, 512)
                            par = kb % 2
                            KB = AR[:, AR_KB + par * 8192: AR_KB + par * 8192 + 8192]
                            lkb = sdma(KB.rearrange("p (h t) -> p h t", t=512),
                                       k_full[2048 * r: 2048 * (r + 1), off: off + 512]
                                       .rearrange("(h p) t -> p h t", p=128),
                                       waits=(kb_use[par], ag2done), sem=sKb[par])
                            vi = kb if qi == 0 else 8 + kb
                            MK = AR[:, AR_MK + par * 2048: AR_MK + par * 2048 + 2048]
                            lmk = sdma(MK, masks_d[vi * 128:(vi + 1) * 128, :],
                                       waits=(mk_use[par],), sem=sMk[par])
                            for j in range(4):
                                parj = j % 2
                                VB = AR[:, AR_VB + parj * 2048: AR_VB + parj * 2048 + 2048]
                                lvb = sdma(VB, v_full[1024 * r + off + 128 * j:
                                                      1024 * r + off + 128 * j + 128, :],
                                           waits=(vb_use[parj], ag2done), sem=sVb[parj])
                                for hp in range(2):
                                    slot = rot[0] % 2
                                    rot[0] += 1
                                    st_flag = (kb == 0 and j == 0)
                                    sp_flag = (kb == nb - 1 and j == 3)
                                    for half in range(2):
                                        hi = hp * 2 + half
                                        h = hg * 4 + hi
                                        pv2 = psc2[:, half * 512: half * 512 + 512]
                                        b.MM(psc2, lambda e, pv2=pv2, h=h, j=j, qi=qi, KB=KB:
                                             e.matmul(pv2,
                                                      KB[:, h * 512 + j * 128: h * 512 + j * 128 + 128],
                                                      SCR[:, SCR_Q + h * 1024 + qi * 512:
                                                          SCR_Q + h * 1024 + qi * 512 + 512],
                                                      start=True, stop=True),
                                             waits=(lkb, qkv_done, mball_done),
                                             start=(half == 0), final=(half == 1))
                                    ex = b.PC(psc2, "act", lambda e, slot=slot, c=gcol + hp:
                                              e.activation(
                                        out=atb[:, slot * 1024: slot * 1024 + 1024],
                                        in_=psc2[:, :], func=AF.Exp,
                                        bias=mhall[:, c:c + 1]),
                                        waits=(atb_use[slot],))
                                    mk = b.E("dve", lambda e, slot=slot, j=j, MK=MK: e.tensor_tensor(
                                        atb[:, slot * 1024: slot * 1024 + 1024]
                                        .rearrange("p (two q) -> p two q", q=512),
                                        atb[:, slot * 1024: slot * 1024 + 1024]
                                        .rearrange("p (two q) -> p two q", q=512),
                                        MK[:, j * 512:(j + 1) * 512].unsqueeze(1)
                                        .broadcast_to([128, 2, 512]), op=ALU.mult),
                                        waits=(ex, lmk), inc=sDV)
                                    if stage == "attn_dbg" and qi == 0 and hg == 0 and kb == 0 and j == 0 and hp == 0:
                                        dbg_atb = dt("dbg_atb", [128, 1024], F32, kind="ExternalOutput").ap()
                                        mk = b.E("gp", lambda e, slot=slot: e.dma_start(
                                            out=dbg_atb[:, :],
                                            in_=atb[:, slot * 1024: slot * 1024 + 1024]),
                                            waits=(mk,), inc=sGP, k=16)
                                    pend.append((slot, hp, st_flag, sp_flag, mk, lvb, VB, parj, par))
                                    if len(pend) > 1:
                                        emit_back(pend.pop(0))
                        while pend:
                            emit_back(pend.pop(0))
                        r_set("mr4", (sPE, sPE.n))
                        for hi in range(4):
                            h = hg * 4 + hi
                            cp = b.PC(psS, "dve", lambda e, hi=hi: e.tensor_copy(
                                rowA[:, :], psS[32 * hi: 32 * hi + 1, :]),
                                waits=(r_wait("rowA"),))
                            rc = b.E("dve", lambda e: e.reciprocal(rowA[:, :], rowA[:, :]), inc=sDV)
                            bm = b.MM(PSC[hi % 2], lambda e, hi=hi: e.matmul(
                                PSC[hi % 2][:, :], onr[:, :], rowA[:, :], start=True, stop=True),
                                waits=(rc,), start=True, final=True)
                            r_set("rowA", bm)
                            iv = b.PC(PSC[hi % 2], "dve", lambda e, hi=hi: e.tensor_copy(
                                bcf[:, 0:512], PSC[hi % 2][:, :]))
                            o_written = b.PC(PO[hi], "dve", lambda e, hi=hi, h=h, qi=qi:
                                             e.tensor_tensor(
                                                 SCR[:, SCR_O + h * 1024 + qi * 512:
                                                     SCR_O + h * 1024 + qi * 512 + 512],
                                                 PO[hi][:, :], bcf[:, 0:512], op=ALU.mult),
                                             waits=(iv,))

                if stage == "attn_dbg":
                    dbg_mh = dt("dbg_mh", [128, 4], F32, kind="ExternalOutput").ap()
                    gdma(dbg_mh[:, :], mh4[:, :], waits=((sDV, sDV.n),))
                    dbg_mhall = dt("dbg_mhall", [128, 16], F32, kind="ExternalOutput").ap()
                    gdma(dbg_mhall[:, :], mhall[:, :], waits=((sDV, sDV.n),))
                    dbg_prb = dt("dbg_prb", [1, 64], F32, kind="ExternalOutput").ap()
                    gdma(dbg_prb[:, :], prb[:, :], waits=((sDV, sDV.n),))
                attn_pe_done = (sPE, sPE.n)
                wo_use = [attn_pe_done, attn_pe_done]
                last = None
                for ob in range(16):
                    par = ob % 2
                    wsl = AR[:, AR_WO + par * 2048: AR_WO + par * 2048 + 2048]
                    lw = sdma(wsl.rearrange("p (hc f) -> p hc f", f=128),
                              wout_b[:, ob * 128:(ob + 1) * 128].rearrange("(hc p) f -> p hc f", p=128),
                              waits=(wo_use[par], wcast_done), sem=sWo[par])
                    for qi in range(2):
                        pp = PSC[(2 * ob + qi) % 2]
                        mmo = None
                        for hc in range(NKC):
                            mmo = b.MM(pp, lambda e, pp=pp, wsl=wsl, hc=hc, qi=qi: e.matmul(
                                pp[:, :], wsl[:, hc * 128:(hc + 1) * 128],
                                SCR[:, SCR_O + hc * 1024 + qi * 512: SCR_O + hc * 1024 + qi * 512 + 512],
                                start=(hc == 0), stop=(hc == NKC - 1)),
                                waits=(lw, o_written), start=(hc == 0), final=(hc == NKC - 1))
                        t1 = b.PC(pp, "dve", lambda e, pp=pp, ob=ob: e.tensor_scalar(
                            out=scb[:, 0:512], in0=pp[:, :], scalar1=bof[:, ob:ob + 1],
                            scalar2=None, op0=ALU.add))
                        last = b.E("dve", lambda e, ob=ob, qi=qi: e.tensor_tensor(
                            X[:, ob * TLOC + qi * 512: ob * TLOC + qi * 512 + 512],
                            X[:, ob * TLOC + qi * 512: ob * TLOC + qi * 512 + 512],
                            scb[:, 0:512], op=ALU.add), inc=sDV)
                    wo_use[par] = mmo
                final_dep = last

                if stage == "full":
                    dep = final_dep
                    arena_free = (sPE, sPE.n)
                    for tg in range(2):
                        ln = emit_ln(tg, AR_XB, AR_XLN, dep)
                        dep = emit_ffn(tg, ln, arena_free=arena_free if tg == 0 else None)
                    final_dep = dep

        trow_use = [None, None]
        last_store = None
        PS4Y = [ps0, ps1, ps2, ps3]
        for tt in range(8):
            par = tt % 2
            trow = ARf[:, par * 2048:(par + 1) * 2048]
            lastcp = None
            for g4 in range(4):
                pg4 = PS4Y[g4]
                for i in range(4):
                    kc = g4 * 4 + i
                    b.MM(pg4, lambda e, kc=kc, tt=tt, pg4=pg4, i=i: e.transpose(
                        pg4[:, i * 128:(i + 1) * 128],
                        X[:, kc * TLOC + tt * 128: kc * TLOC + tt * 128 + 128], idf[:, :]),
                        waits=(final_dep, cur(sStY) if g4 == 0 and i == 0 else None),
                        start=(i == 0), final=(i == 3))
                lastcp = b.PC(pg4, "dve", lambda e, g4=g4, trow=trow, pg4=pg4: e.tensor_copy(
                    trow[:, g4 * 512:(g4 + 1) * 512], pg4[:, :]))
            last_store = gdma(y_out[tt * 128:(tt + 1) * 128, :], trow, waits=(lastcp,), sem=sStY)
            trow_use[par] = last_store
        b.E("gp", None, waits=(last_store,))

        block = es.enter_context(nc.Block())

        @block.tensor
        def _(e):
            for fn in b.sched["pe"]:
                fn(e)

        @block.vector
        def _(e):
            for fn in b.sched["dve"]:
                fn(e)

        @block.scalar
        def _(e):
            for fn in b.sched["act"]:
                fn(e)

        @block.gpsimd
        def _(e):
            for fn in b.sched["gp"]:
                fn(e)

        @block.sync
        def _(e):
            for fn in b.sched["sp"]:
                fn(e)

    nc._b_trace = b.trace
    nc._dbgi = dbgi
    nc._sem_ids = {"sPE": id(sPE), "sDV": id(sDV), "sAC": id(sAC)}
    return nc


def check_deadlock(nc):
    trace = nc._b_trace
    heads = {k: 0 for k in trace}
    ctr = {}
    progress = True
    while progress:
        progress = False
        for eng, items in trace.items():
            while heads[eng] < len(items):
                wids, inc, _tag = items[heads[eng]]
                if all(ctr.get(sid, 0) >= v for (sid, v) in wids):
                    if inc is not None:
                        ctr[inc[0]] = ctr.get(inc[0], 0) + inc[1]
                    heads[eng] += 1
                    progress = True
                else:
                    break
    stuck = {k: (heads[k], len(v)) for k, v in trace.items() if heads[k] < len(v)}
    if not stuck:
        return None
    rep = {"stuck": stuck}
    for eng in stuck:
        wids, inc, tag = trace[eng][heads[eng]]
        rep[eng] = [(sid, v, ctr.get(sid, 0)) for (sid, v) in wids if ctr.get(sid, 0) < v]
    return rep


def check_psum_hazards(nc):
    trace = nc._b_trace
    engs = list(trace.keys())
    eidx = {e: i for i, e in enumerate(engs)}
    heads = {k: 0 for k in trace}
    ctr = {}
    producers = {}
    VC = {e: [None] * len(trace[e]) for e in engs}
    order = []
    progress = True
    while progress:
        progress = False
        for eng, items in trace.items():
            while heads[eng] < len(items):
                i = heads[eng]
                wids, inc, tag = items[i]
                if not all(ctr.get(sid, 0) >= v for (sid, v) in wids):
                    break
                vc = [-1] * len(engs)
                if i > 0:
                    vc = list(VC[eng][i - 1])
                vc[eidx[eng]] = i
                for (sid, v) in wids:
                    p = producers.get(sid)
                    for (pv, pe_, pi_) in p:
                        if pv >= v:
                            pvc = VC[pe_][pi_]
                            vc = [max(a, b) for a, b in zip(vc, pvc)]
                            break
                if inc is not None:
                    sid, k = inc
                    ctr[sid] = ctr.get(sid, 0) + k
                    producers.setdefault(sid, []).append((ctr[sid], eng, i))
                VC[eng][i] = vc
                heads[eng] += 1
                order.append((eng, i))
                progress = True
    banks = {}
    for eng, items in trace.items():
        for i, (wids, inc, tag) in enumerate(items):
            if tag is None:
                continue
            kind, bid = tag
            banks.setdefault(bid, {"w": [], "r": []})[kind].append((eng, i))
    conflicts = []
    for bid, d in banks.items():
        for (reng, ri) in d["r"]:
            rvc = VC[reng][ri]
            for (weng, wi) in d["w"]:
                if weng == reng:
                    continue
                wvc = VC[weng][wi]
                w_before_r = rvc[eidx[weng]] >= wi
                r_before_w = wvc[eidx[reng]] >= ri
                if not (w_before_r or r_before_w):
                    conflicts.append((bid, (weng, wi), (reng, ri)))
                    if len(conflicts) > 20:
                        return conflicts
    return conflicts


def _host_prep(inputs):
    x = np.asarray(inputs["L0_attn_x"], np.float32).reshape(S, D)
    W_qkv = np.asarray(inputs["L0_attn_W_qkv"], np.float32)
    b_qkv = np.asarray(inputs["L0_attn_b_qkv"], np.float32)
    W_out = np.asarray(inputs["L0_attn_W_out"], np.float32)
    b_out = np.asarray(inputs["L0_attn_b_out"], np.float32)
    w_gu = np.asarray(inputs["L0_ffn_w_gate_up"], np.float32)
    w_pr = np.asarray(inputs["L0_ffn_w_proj"], np.float32)

    wqkv_t = np.ascontiguousarray(W_qkv.T)
    wout_t = np.ascontiguousarray(W_out.T)
    bqkv_fm = np.ascontiguousarray(b_qkv.reshape(48, 128).T)
    bqkv_v = np.ascontiguousarray(b_qkv[2 * D:].reshape(1, D))
    bout_fm = np.ascontiguousarray(b_out.reshape(16, 128).T)
    ones_c = np.ones((128, 1), np.float32)
    negr = -np.ones((1, 128), np.float32)
    ones_r = np.ones((1, 128), np.float32)
    ident = np.eye(128, dtype=np.float32)
    cst_in = np.broadcast_to(np.array([1.0 / 64, 1e-8, 0.5, -0.5, 1.0 / D, -1.0,
                                       EPS_LN, 1.0 / HFF, EPS_RMS, 1.0002], np.float32),
                             (128, 10)).copy()

    diag = np.zeros((4, 128, 512), np.float32)
    kk = np.arange(128)[:, None]
    qq = np.arange(512)[None, :]
    for j in range(4):
        diag[j] = (128 * j + kk <= qq).astype(np.float32)

    in_maps = []
    for c in range(NCORES):
        a_ch, b_ch = c, 15 - c
        rows = np.r_[a_ch * 512:(a_ch + 1) * 512, b_ch * 512:(b_ch + 1) * 512]
        x_c = np.ascontiguousarray(x[rows])
        wgu_sl = np.ascontiguousarray(w_gu[:, 256 * c: 256 * (c + 1)])
        wpr_sl = np.ascontiguousarray(w_pr[:, 1024 * c: 1024 * (c + 1)])
        masks = np.zeros((NB[0] + NB[1], 128, 2048), np.float32)
        for qi, (mq, nb, base) in enumerate(((a_ch, NB[0], 0), (b_ch, NB[1], NB[0]))):
            for kb in range(nb):
                for j in range(4):
                    if kb < mq:
                        masks[base + kb, :, j * 512:(j + 1) * 512] = 1.0
                    elif kb == mq:
                        masks[base + kb, :, j * 512:(j + 1) * 512] = diag[j]
        in_maps.append(dict(
            x_loc=x_c, wqkv_t=wqkv_t, wout_t=wout_t, bqkv_fm=bqkv_fm, bqkv_v=bqkv_v,
            bout_fm=bout_fm, wgu_sl=wgu_sl, wpr_sl=wpr_sl,
            masks_d=masks.reshape((NB[0] + NB[1]) * 128, 2048).astype(bf16),
            ones_c=ones_c, ones_r=ones_r, ident=ident, cst_in=cst_in, negr_in=negr))
    return in_maps


def kernel(**inputs):
    in_maps = _host_prep(inputs)
    nc = build_nc("full")
    res = run_bass_kernel_spmd(nc, in_maps, core_ids=list(range(NCORES)))
    out = np.zeros((S, D), np.float32)
    for c in range(NCORES):
        y = np.asarray(res.results[c]["y_out"], np.float32)
        out[c * 512:(c + 1) * 512] = y[:512]
        out[(15 - c) * 512:(16 - c) * 512] = y[512:]
    return out.reshape(1, S, D)
